# version 45
# baseline (speedup 1.0000x reference)
import numpy as np
import ml_dtypes
import concourse.bass as bass
import concourse.mybir as mybir
from concourse.bass_utils import run_bass_kernel_spmd

F32 = mybir.dt.float32
BF16 = mybir.dt.bfloat16
FP8 = mybir.dt.float8e4
AF = mybir.ActivationFunctionType
OP = mybir.AluOpType
BF = ml_dtypes.bfloat16

NCORES = 8
L = 2048
DM = 1024
DI = 2048
NDM = DM // 128
NDI = DI // 128
NT = 2
TO = 1024
EPS = 1e-5

ENGINES = ("sync", "gpsimd", "tensor", "scalar", "vector")

PREV3 = {0: 15, 1: 13, 2: 14}


class Sched:
    def __init__(self):
        self.q = {e: [] for e in ENGINES}
        self.cnt = {}

    def sem(self, name):
        self.cnt[name] = 0

    def wait(self, eng, sem, val):
        if val and val > 0:
            self.q[eng].append(("w", sem, int(val)))

    def emit(self, eng, fn, sem=None, amt=1, drain=False):
        self.q[eng].append(("o", fn, sem, amt, drain))
        if sem is not None:
            self.cnt[sem] += amt
            return self.cnt[sem]
        return None


def build_nc():
    nc = bass.Bass("TRN2", target_bir_lowering=False)

    dx = nc.dram_tensor("x", [DM, L], F32, kind="ExternalInput").ap()
    dw1a = nc.dram_tensor("w1a", [16, 128, 1536], F32, kind="ExternalInput").ap()
    dw1b = nc.dram_tensor("w1b", [16, 128, 1024], F32, kind="ExternalInput").ap()
    dw2 = nc.dram_tensor("w2", [16, 128, 2048], F32, kind="ExternalInput").ap()
    dwbc = nc.dram_tensor("wbc", [128, 512], F32, kind="ExternalInput").ap()
    dw3 = nc.dram_tensor("w3", [8, 128, 2048], F32, kind="ExternalInput").ap()
    dfk = nc.dram_tensor("fk", [8, 128, 1024], BF16, kind="ExternalInput").ap()
    dcb = nc.dram_tensor("cb", [128, 16], F32, kind="ExternalInput").ap()
    ddm = nc.dram_tensor("dmat", [128, 16], F32, kind="ExternalInput").ap()
    dfb = nc.dram_tensor("fbias", [128, 8], BF16, kind="ExternalInput").ap()
    dsc = nc.dram_tensor("scales", [1, 2], F32, kind="ExternalInput").ap()
    donr = nc.dram_tensor("onesr", [1, 128], F32, kind="ExternalInput").ap()
    dngr = nc.dram_tensor("negr", [1, 128], F32, kind="ExternalInput").ap()
    doncf = nc.dram_tensor("onescf", [128, 1], F32, kind="ExternalInput").ap()
    doncb = nc.dram_tensor("onescb", [128, 1], BF16, kind="ExternalInput").ap()
    dc1cb = nc.dram_tensor("c1cb", [128, 1], BF16, kind="ExternalInput").ap()
    dout = nc.dram_tensor("out", [DM, L], F32, kind="ExternalOutput").ap()

    from contextlib import ExitStack
    with ExitStack() as st:
        en = st.enter_context
        xbuf = en(nc.sbuf_tensor("xbuf", [128, 8192], F32))
        xlnb = en(nc.sbuf_tensor("xlnb", [128, 8192], BF16))
        xcb = en(nc.sbuf_tensor("xcb", [128, 16384], BF16))
        szb = en(nc.sbuf_tensor("szb", [128, 16384], BF16))
        xbs = en(nc.sbuf_tensor("xbs", [128, 4 * 1028], BF16))
        halo = en(nc.sbuf_tensor("halo", [128, 48], BF16))
        qb = en(nc.sbuf_tensor("qb", [128, 8192], FP8))
        kqb = en(nc.sbuf_tensor("kqb", [128, 8192], FP8))
        wm1 = en(nc.sbuf_tensor("wm1", [128, 3 * 1536], BF16))
        wm2 = en(nc.sbuf_tensor("wm2", [128, 3 * 2048], BF16))
        wm3 = en(nc.sbuf_tensor("wm3", [128, 2 * 2048], BF16))
        wbcb = en(nc.sbuf_tensor("wbcb", [128, 512], BF16))
        dscr = en(nc.sbuf_tensor("dscr", [128, 2048], F32))
        yacc = en(nc.sbuf_tensor("yacc", [128, 1], F32))
        yscr = en(nc.sbuf_tensor("yscr", [128, 1024], F32))
        lnscr = en(nc.sbuf_tensor("lnscr", [128, 1024], F32))
        xbfs = en(nc.sbuf_tensor("xbfs", [128, 2048], BF16))
        sqbfs = en(nc.sbuf_tensor("sqbfs", [128, 2048], BF16))
        outst = en(nc.sbuf_tensor("outst", [128, 2048], F32))
        statb = en(nc.sbuf_tensor("statb", [128, 1024], F32))
        ffns = en(nc.sbuf_tensor("ffns", [128, 1024], BF16))
        onr = en(nc.sbuf_tensor("onr", [33, 128], F32))
        ngr = en(nc.sbuf_tensor("ngr", [33, 128], F32))
        oncf = en(nc.sbuf_tensor("oncf", [128, 1], F32))
        oncb = en(nc.sbuf_tensor("oncb", [128, 1], BF16))
        c1cb = en(nc.sbuf_tensor("c1cs", [128, 1], BF16))
        cbs = en(nc.sbuf_tensor("cbs", [128, 16], F32))
        dms = en(nc.sbuf_tensor("dms", [128, 16], F32))
        fbbf = en(nc.sbuf_tensor("fbbf", [128, 8], BF16))
        fbs = en(nc.sbuf_tensor("fbs", [128, 8], F32))
        scs = en(nc.sbuf_tensor("scs", [128, 4], F32))
        srow = en(nc.sbuf_tensor("srow", [1, 8], F32))
        B = [en(nc.psum_tensor(f"B{i}", [128, 512], F32)) for i in range(8)]
        sems = {}
        sem_names = (["CD", "FK", "WB", "PE", "SC", "VE", "VCH", "SCH"]
                     + [f"W1S{i}" for i in range(3)]
                     + [f"W2S{i}" for i in range(3)]
                     + [f"W3S{i}" for i in range(2)]
                     + [f"XL{i}" for i in range(8)]
                     + [f"OD{i}" for i in range(8)])
        for s in sem_names:
            sems[s] = en(nc.semaphore(s))
        block = en(nc.Block())

        S = Sched()
        for s in sems:
            S.sem(s)

        wd_cnt = {}
        m1_pe, conv_pe, m2_pe, m3_pe, m4_pe = {}, {}, {}, {}, {}
        evac_sc, silu_sc, zevac_sc, sp_sc, m4evac_sc, quant_sc = {}, {}, {}, {}, {}, {}
        apply_ve, t1_ve, y2_ve, resid_ve, final_ve = {}, {}, {}, {}, {}
        hrest_ve, hsave_ve = {}, {}
        store_od, xload_xd, cast_sc, castv_ve = {}, {}, {}, {}
        stats_pe, stats2_pe, bcast_pe = {}, {}, {}

        def dma(eng, dst, src, sem):
            return S.emit(eng, lambda E, d=dst, s_=src: E.dma_start(out=d, in_=s_), sem, 16)

        dma("sync", c1cb[:, :], dc1cb, "CD")
        xload_xd[(0, 0)] = dma(
            "sync", xbuf[:, 0:1024], dx[0:128, 0:TO], "XL0")
        dma("sync", onr[0:1, :], donr, "CD")
        dma("sync", onr[32:33, :], donr, "CD")
        dma("sync", ngr[0:1, :], dngr, "CD")
        dma("sync", ngr[32:33, :], dngr, "CD")
        dma("sync", oncf[:, :], doncf, "CD")
        dma("sync", oncb[:, :], doncb, "CD")
        dma("sync", cbs[:, :], dcb, "CD")
        dma("sync", dms[:, :], ddm, "CD")
        dma("sync", fbbf[:, :], dfb, "CD")
        xd_consts = dma("sync", srow[0:1, 0:2], dsc, "CD")
        for c in range(1, NDM):
            xload_xd[(0, c)] = dma(
                "sync", xbuf[:, c * 1024:(c + 1) * 1024],
                dx[c * 128:(c + 1) * 128, 0:TO], f"XL{c}")
        for i in range(8):
            area, off = (wm2, (i % 4) * 1024) if i < 4 else (wm3, (i - 4) * 1024)
            xd_setup = dma("sync", area[:, off:off + 1024], dfk[i, :, :], "FK")

        S.wait("vector", "CD", xd_consts)
        S.emit("vector", lambda E: E.tensor_copy(fbs[:, :], fbbf[:, :]))
        S.emit("vector", lambda E: E.reciprocal(srow[0:1, 2:3], srow[0:1, 1:2]))
        S.emit("vector", lambda E: E.reciprocal(srow[0:1, 3:4], srow[0:1, 0:1]))
        ve_rows = S.emit("vector", lambda E: E.tensor_tensor(
            srow[0:1, 4:5], srow[0:1, 0:1], srow[0:1, 1:2], OP.mult), "VE")
        S.wait("tensor", "VE", ve_rows)
        S.emit("tensor", lambda E: E.matmul(B[0][:, 0:1], onr[0:1, :], srow[0:1, 3:4], start=True, stop=True))
        S.emit("tensor", lambda E: E.matmul(B[0][:, 1:2], onr[0:1, :], srow[0:1, 4:5], start=True, stop=True))
        pe_sc = S.emit("tensor", lambda E: E.matmul(B[0][:, 2:3], onr[0:1, :], srow[0:1, 2:3], start=True, stop=True), "PE")
        S.wait("vector", "PE", pe_sc)
        S.emit("vector", lambda E: E.tensor_copy(scs[:, 0:3], B[0][:, 0:3]), "VE")
        S.emit("vector", lambda E: E.memset(halo[:, :], 0.0))
        ve_setup = S.emit("vector", lambda E: E.memset(xbs[:, :], 0.0), "VE")
        wd_wbc = dma("gpsimd", wbcb[:, :], dwbc, "WB")
        sc_kq = 0

        def m1_rhs(d, h):
            return xlnb[:, d * 1024 + h * 512: d * 1024 + (h + 1) * 512]

        def xc_rhs(d, h):
            return xcb[:, d * 1024 + h * 512: d * 1024 + (h + 1) * 512]

        def sz_rhs(d, h):
            return szb[:, d * 1024 + h * 512: d * 1024 + (h + 1) * 512]

        def ln_finalize(t, ln):
            sp_dict = stats_pe if ln == 0 else stats2_pe
            S.wait("vector", "PE", sp_dict[(t, NDM - 1)])
            S.wait("scalar", "PE", sp_dict[(t, NDM - 1)])

            def warm(n, gate=None):
                if gate is not None:
                    S.wait("tensor", gate[0], gate[1])
                for _ in range(n):
                    S.emit("tensor", lambda E: E.matmul(
                        B[6][64:65, :], c1cb[:, 0:1], xbfs[:, 0:512],
                        start=True, stop=True, skip_group_check=True,
                        tile_position=(0, 64)))

            bc = {}
            for i, hr in enumerate((0, 32)):
                sq_m = S.emit("scalar", lambda E, hr=hr: E.activation(
                    statb[hr:hr + 1, 0:512], B[4][hr:hr + 1, :], AF.Square), "SC")
                S.emit("vector", lambda E, hr=hr: E.tensor_scalar(
                    statb[hr:hr + 1, 512:1024], B[5][hr:hr + 1, :], EPS, None, OP.add))
                S.wait("vector", "SC", sq_m)
                v_var = S.emit("vector", lambda E, hr=hr: E.tensor_tensor(
                    statb[hr:hr + 1, 0:512], statb[hr:hr + 1, 512:1024],
                    statb[hr:hr + 1, 0:512], OP.subtract), "VE", drain=True)
                if i == 0:
                    warm(3, ("SC", sq_m))
                S.wait("scalar", "VE", v_var)
                S.emit("scalar", lambda E, hr=hr: E.activation(
                    statb[hr:hr + 1, 512:1024], statb[hr:hr + 1, 0:512], AF.Ln),
                    drain=True)
                s_a = S.emit("scalar", lambda E, hr=hr: E.activation(
                    statb[hr:hr + 1, 0:512], statb[hr:hr + 1, 512:1024], AF.Exp, scale=-0.5),
                    "SC", drain=True)
                S.wait("vector", "SC", s_a)
                v_d = S.emit("vector", lambda E, hr=hr: E.tensor_tensor(
                    statb[hr:hr + 1, 512:1024], B[4][hr:hr + 1, :],
                    statb[hr:hr + 1, 0:512], OP.mult), "VE", drain=True)
                S.wait("tensor", "VE", v_d)
                if i == 0:
                    S.emit("tensor", lambda E: E.matmul(
                        B[6][:, :], onr[0:1, :], statb[0:1, 0:512], start=True, stop=True))
                    bc[0] = S.emit("tensor", lambda E: E.matmul(
                        B[7][:, :], ngr[0:1, :], statb[0:1, 512:1024], start=True, stop=True), "PE")
                else:
                    S.emit("tensor", lambda E: E.matmul(
                        B[4][:, :], onr[32:33, :], statb[32:33, 0:512], start=True, stop=True))
                    bc[1] = S.emit("tensor", lambda E: E.matmul(
                        B[5][:, :], ngr[32:33, :], statb[32:33, 512:1024], start=True, stop=True), "PE")
            return bc

        def w1_dma(t, c):
            slot = c % 3
            if c >= 3:
                cc = c - 3
                S.wait("gpsimd", "PE", conv_pe[(t, cc)] if cc < 16 else m1_pe[(t, cc)])
            elif t > 0:
                S.wait("gpsimd", "PE", m1_pe[(t - 1, {0: 30, 1: 31, 2: 29}[c])])
            if c < 16:
                src = dw1a[c, :, :]
                dst = wm1[:, slot * 1536: slot * 1536 + 1536]
            else:
                src = dw1b[c - 16, :, :]
                dst = wm1[:, slot * 1536: slot * 1536 + 1024]
            wd_cnt[(t, "w1", c)] = dma("gpsimd", dst, src, f"W1S{slot}")

        def ln1_xload(t, c):
            S.wait("sync", "VE", final_ve[(t - 1, c)])
            xload_xd[(t, c)] = dma(
                "sync", xbuf[:, c * 1024:(c + 1) * 1024],
                dx[c * 128:(c + 1) * 128, t * TO:(t + 1) * TO], f"XL{c}")

        def ln1_cast(t, c):
            slot = c % 2
            S.wait("scalar", f"XL{c}", xload_xd[(t, c)])
            S.wait("vector", f"XL{c}", xload_xd[(t, c)])
            if c >= 2:
                S.wait("scalar", "PE", stats_pe[(t, c - 2)])
                S.wait("vector", "PE", stats_pe[(t, c - 2)])
            elif t > 0:
                S.wait("scalar", "PE", stats2_pe[(t - 1, 6 + c)])
                S.wait("vector", "PE", stats2_pe[(t - 1, 6 + c)])
            castv_ve[(t, c)] = S.emit("vector", lambda E, c=c, s=slot: E.tensor_copy(
                xbfs[:, s * 1024:(s + 1) * 1024],
                xbuf[:, c * 1024:(c + 1) * 1024]), "VE")
            cast_sc[(t, c)] = S.emit("scalar", lambda E, c=c, s=slot: E.activation(
                sqbfs[:, s * 1024:(s + 1) * 1024],
                xbuf[:, c * 1024:(c + 1) * 1024], AF.Square), "SC")

        def ln1_statmm(t, c):
            slot = c % 2
            S.wait("tensor", "SC", cast_sc[(t, c)])
            S.wait("tensor", "VE", castv_ve[(t, c)])
            S.wait("tensor", "CD", xd_consts)
            if t > 0 and c == 0:
                S.wait("tensor", "VE", apply_ve[(t - 1, 1, NDM - 1, 1)])
            st_, sp_ = (c == 0), (c == NDM - 1)
            S.emit("tensor", lambda E, s=slot, st=st_, sp=sp_: E.matmul(
                B[4][0:1, :], c1cb[:, 0:1], xbfs[:, s * 1024:s * 1024 + 512], start=st, stop=sp))
            S.emit("tensor", lambda E, s=slot, st=st_, sp=sp_: E.matmul(
                B[4][32:33, :], c1cb[:, 0:1], xbfs[:, s * 1024 + 512:(s + 1) * 1024], start=st, stop=sp,
                tile_position=(0, 32)))
            S.emit("tensor", lambda E, s=slot, st=st_, sp=sp_: E.matmul(
                B[5][0:1, :], c1cb[:, 0:1], sqbfs[:, s * 1024:s * 1024 + 512], start=st, stop=sp))
            stats_pe[(t, c)] = S.emit("tensor", lambda E, s=slot, st=st_, sp=sp_: E.matmul(
                B[5][32:33, :], c1cb[:, 0:1], sqbfs[:, s * 1024 + 512:(s + 1) * 1024], start=st, stop=sp,
                tile_position=(0, 32)), "PE")

        def ln1_tail(t):
            bc = ln_finalize(t, 0)
            bcast_pe[(t, 0)] = bc[1]
            for h in range(2):
                aB = B[6] if h == 0 else B[4]
                dB = B[7] if h == 0 else B[5]
                for cp in range(0, NDM, 2):
                    S.wait("vector", "PE", bc[h])
                    if t > 0 and cp == 0 and h == 0:
                        S.wait("vector", "SC", quant_sc[(t - 1, NDM - 1, 1)])
                    for c in (cp, cp + 1):
                        S.emit("vector", lambda E, c=c, h=h, aB=aB, r=(c % 2) * 512: E.tensor_tensor(
                            lnscr[:, r:r + 512],
                            xbuf[:, c * 1024 + h * 512: c * 1024 + (h + 1) * 512],
                            aB[:, :], OP.mult), drain=(c == cp))
                    for c in (cp, cp + 1):
                        apply_ve[(t, 0, c, h)] = S.emit(
                            "vector", lambda E, c=c, h=h, dB=dB, r=(c % 2) * 512: E.tensor_tensor(
                                xlnb[:, c * 1024 + h * 512: c * 1024 + (h + 1) * 512],
                                lnscr[:, r:r + 512], dB[:, :], OP.add), "VE", drain=(c == cp))

        for t in range(NT):
            if t == 0:
                for c in range(3):
                    w1_dma(0, c)
                for c in range(NDM):
                    ln1_cast(0, c)
                    ln1_statmm(0, c)
                ln1_tail(0)

            bcproj_pe = v_prod = sc_cm = None
            bcred_pe = v_bcrows = bcB_pe = None
            def emit_bc_part1():
                nonlocal bcproj_pe, v_prod, sc_cm
                S.wait("tensor", "SC", silu_sc[(t, 15)])
                S.wait("tensor", "WB", wd_wbc)
                r = None
                for d in range(NDI):
                    st_, sp_ = (d == 0), (d == NDI - 1)
                    S.emit("tensor", lambda E, d=d, st=st_, sp=sp_: E.matmul(
                        B[4][0:16, :], wbcb[:, d * 32:d * 32 + 16], xc_rhs(d, 0), start=st, stop=sp,
                        skip_group_check=True))
                    S.emit("tensor", lambda E, d=d, st=st_, sp=sp_: E.matmul(
                        B[4][32:48, :], wbcb[:, d * 32:d * 32 + 16], xc_rhs(d, 1), start=st, stop=sp,
                        tile_position=(0, 32), skip_group_check=True))
                    S.emit("tensor", lambda E, d=d, st=st_, sp=sp_: E.matmul(
                        B[5][0:16, :], wbcb[:, d * 32 + 16:(d + 1) * 32], xc_rhs(d, 0), start=st, stop=sp,
                        skip_group_check=True))
                    r = S.emit("tensor", lambda E, d=d, st=st_, sp=sp_: E.matmul(
                        B[5][32:48, :], wbcb[:, d * 32 + 16:(d + 1) * 32], xc_rhs(d, 1), start=st, stop=sp,
                        tile_position=(0, 32), skip_group_check=True), "PE" if sp_ else None)
                bcproj_pe = r
                S.wait("scalar", "PE", bcproj_pe)
                sc_cm = S.emit("scalar", lambda E: E.activation(
                    yscr[0:48, 512:1024], B[5][0:48, :], AF.Copy), "SC")
                S.wait("vector", "SC", sc_cm)
                S.emit("vector", lambda E: E.tensor_tensor(
                    yscr[0:16, 0:512], B[4][0:16, :], yscr[0:16, 512:1024], OP.mult), drain=True)
                v_prod = S.emit("vector", lambda E: E.tensor_tensor(
                    yscr[32:48, 0:512], B[4][32:48, :], yscr[32:48, 512:1024], OP.mult), "VE")
            def emit_bc_part2():
                nonlocal bcred_pe, v_bcrows, bcB_pe
                S.wait("tensor", "VE", v_prod)
                S.emit("tensor", lambda E: E.matmul(
                    B[6][0:1, 0:512], oncf[0:16, 0:1], yscr[0:16, 0:512], start=True, stop=True))
                bcred_pe = S.emit("tensor", lambda E: E.matmul(
                    B[6][32:33, 0:512], oncf[32:48, 0:1], yscr[32:48, 0:512], start=True, stop=True,
                    tile_position=(32, 32)), "PE")
                S.wait("scalar", "PE", bcred_pe)
                S.emit("scalar", lambda E: E.activation(lnscr[0:1, 0:512], B[6][0:1, :], AF.Copy))
                v_bcrows = S.emit("scalar", lambda E: E.activation(
                    lnscr[32:33, 0:512], B[6][32:33, :], AF.Copy), "SC")
                S.wait("tensor", "SC", v_bcrows)
                S.emit("tensor", lambda E: E.matmul(
                    B[6][:, :], onr[0:1, :], lnscr[0:1, 0:512], start=True, stop=True))
                bcB_pe = S.emit("tensor", lambda E: E.matmul(
                    B[7][:, :], onr[32:33, :], lnscr[32:33, 0:512], start=True, stop=True), "PE")

            pending_conv = []
            for c in range(32):
                slot = c % 3
                pair = (B[0], B[1]) if c % 2 == 0 else (B[2], B[3])
                S.wait("tensor", f"W1S{c % 3}", wd_cnt[(t, "w1", c)])
                if c >= 2:
                    S.wait("tensor", "SC", evac_sc.get((t, c - 2)) or zevac_sc.get((t, c - 2)))
                elif t > 0:
                    S.wait("tensor", "SC", m4evac_sc[(t - 1, 6 + c)])
                r = None
                if c == 0:
                    for h in range(2):
                        for d in range(NDM):
                            st_, sp_ = (d == 0), (d == NDM - 1)
                            S.wait("tensor", "VE", apply_ve[(t, 0, d, h)])
                            lhs = wm1[:, slot * 1536 + d * 128: slot * 1536 + (d + 1) * 128]
                            r = S.emit("tensor", lambda E, l=lhs, p=pair[h], d=d, h=h, st=st_, sp=sp_: E.matmul(
                                p[:, :], l, m1_rhs(d, h), start=st, stop=sp),
                                "PE" if (sp_ and h == 1) else None)
                else:
                    for d in range(NDM):
                        st_, sp_ = (d == 0), (d == NDM - 1)
                        S.wait("tensor", "VE", apply_ve[(t, 0, d, 0)])
                        lhs = wm1[:, slot * 1536 + d * 128: slot * 1536 + (d + 1) * 128]
                        S.emit("tensor", lambda E, l=lhs, p=pair[0], d=d, st=st_, sp=sp_: E.matmul(
                            p[:, :], l, m1_rhs(d, 0), start=st, stop=sp))
                        S.wait("tensor", "VE", apply_ve[(t, 0, d, 1)])
                        r = S.emit("tensor", lambda E, l=lhs, p=pair[1], d=d, st=st_, sp=sp_: E.matmul(
                            p[:, :], l, m1_rhs(d, 1), start=st, stop=sp), "PE" if sp_ else None)
                m1_pe[(t, c)] = r

                S.wait("scalar", "PE", m1_pe[(t, c)])
                if c < 16:
                    base = (c % 4) * 1028
                    if c >= 4:
                        S.wait("scalar", "PE", conv_pe[(t, c - 4)])
                        S.wait("scalar", "VE", hsave_ve[(t, c - 4)])
                    elif t > 0:
                        S.wait("scalar", "PE", conv_pe[(t - 1, 12 + c)])
                        S.wait("scalar", "VE", hsave_ve[(t - 1, 12 + c)])
                    S.emit("scalar", lambda E, p=pair[0], b=base: E.activation(
                        xbs[:, b + 3: b + 515], p[:, :], AF.Copy))
                    evac_sc[(t, c)] = S.emit("scalar", lambda E, p=pair[1], b=base: E.activation(
                        xbs[:, b + 515: b + 1027], p[:, :], AF.Copy), "SC")
                    if c >= 4:
                        S.wait("vector", "PE", conv_pe[(t, c - 4)])
                    elif t > 0:
                        S.wait("vector", "PE", conv_pe[(t - 1, 12 + c)])
                    hrest_ve[(t, c)] = S.emit("vector", lambda E, c=c, b=base: E.tensor_copy(
                        xbs[:, b: b + 3], halo[:, c * 3:(c + 1) * 3]), "VE")
                    S.wait("vector", "SC", evac_sc[(t, c)])
                    hsave_ve[(t, c)] = S.emit("vector", lambda E, c=c, b=base: E.tensor_copy(
                        halo[:, c * 3:(c + 1) * 3], xbs[:, b + 1024: b + 1027]), "VE", drain=True)

                    def emit_conv(c=c, base=base, slot=slot, t=t):
                        cpair = (B[4], B[5]) if c % 2 == 0 else (B[6], B[7])
                        S.wait("tensor", "SC", evac_sc[(t, c)])
                        S.wait("tensor", "VE", hrest_ve[(t, c)])
                        if c >= 2:
                            S.wait("tensor", "SC", silu_sc[(t, c - 2)])
                        r = None
                        for k in range(4):
                            st_, sp_ = (k == 0), (k == 3)
                            lhs = wm1[:, slot * 1536 + 1024 + k * 128: slot * 1536 + 1024 + (k + 1) * 128]
                            S.emit("tensor", lambda E, l=lhs, p=cpair[0], b=base, k=k, st=st_, sp=sp_: E.matmul(
                                p[:, :], l, xbs[:, b + k: b + k + 512], start=st, stop=sp))
                            r = S.emit("tensor", lambda E, l=lhs, p=cpair[1], b=base, k=k, st=st_, sp=sp_: E.matmul(
                                p[:, :], l, xbs[:, b + 512 + k: b + 1024 + k], start=st, stop=sp),
                                "PE" if sp_ else None)
                        conv_pe[(t, c)] = r
                        S.wait("scalar", "PE", conv_pe[(t, c)])
                        if t > 0 and c == 0:
                            S.wait("scalar", "PE", m2_pe[(t - 1, NDI - 1)])
                        S.emit("scalar", lambda E, p=cpair[0], c=c: E.activation(
                            xcb[:, c * 1024: c * 1024 + 512], p[:, :], AF.Silu, bias=cbs[:, c:c + 1]))
                        silu_sc[(t, c)] = S.emit("scalar", lambda E, p=cpair[1], c=c: E.activation(
                            xcb[:, c * 1024 + 512: (c + 1) * 1024], p[:, :], AF.Silu, bias=cbs[:, c:c + 1]), "SC")
                        if t == 0 and c % 2 == 1:
                            i = c // 2
                            area, off = (wm2, (i % 4) * 1024) if i < 4 else (wm3, (i - 4) * 1024)
                            S.wait("scalar", "FK", xd_setup)
                            S.wait("scalar", "VE", ve_setup)
                            nonlocal sc_kq
                            sc_kq = S.emit("scalar", lambda E, a=area, o=off, i=i: E.activation(
                                kqb[:, i * 1024:(i + 1) * 1024], a[:, o:o + 1024], AF.Copy,
                                scale=scs[:, 2:3]), "SC")
                        if c + 3 < 32:
                            w1_dma(t, c + 3)

                    pending_conv.append(emit_conv)
                    if len(pending_conv) > 1:
                        pending_conv.pop(0)()
                else:
                    zc = c - 16
                    if t > 0 and zc == 0:
                        S.wait("scalar", "PE", m3_pe[(t - 1, 7)])
                    S.emit("scalar", lambda E, p=pair[0], zc=zc: E.activation(
                        szb[:, zc * 1024: zc * 1024 + 512], p[:, :], AF.Silu))
                    zevac_sc[(t, c)] = S.emit("scalar", lambda E, p=pair[1], zc=zc: E.activation(
                        szb[:, zc * 1024 + 512: (zc + 1) * 1024], p[:, :], AF.Silu), "SC")
                    while pending_conv:
                        pending_conv.pop(0)()
                    if c == 16:
                        emit_bc_part1()
                    if c + 3 < 32:
                        w1_dma(t, c + 3)

            emit_bc_part2()

            for c in range(NDI):
                if c >= 3:
                    S.wait("gpsimd", "PE", m2_pe[(t, c - 3)])
                elif t == 0 and c == 0:
                    S.wait("gpsimd", "SC", sc_kq)
                elif t > 0:
                    S.wait("gpsimd", "PE", m2_pe[(t - 1, 13 + c)])
                wd_cnt[(t, "w2", c)] = dma(
                    "gpsimd", wm2[:, (c % 3) * 2048:(c % 3 + 1) * 2048], dw2[c, :, :], f"W2S{c % 3}")

                pair = (B[0], B[1]) if c % 2 == 0 else (B[2], B[3])
                S.wait("tensor", f"W2S{c % 3}", wd_cnt[(t, "w2", c)])
                if c >= 2:
                    S.wait("tensor", "SC", sp_sc[(t, c - 2)])
                else:
                    S.wait("tensor", "SC", zevac_sc[(t, 30 + c)])
                r = None
                for d in range(NDI):
                    st_, sp_ = (d == 0), (d == NDI - 1)
                    lhs = wm2[:, (c % 3) * 2048 + d * 128: (c % 3) * 2048 + (d + 1) * 128]
                    S.emit("tensor", lambda E, l=lhs, p=pair[0], d=d, st=st_, sp=sp_: E.matmul(
                        p[:, :], l, xc_rhs(d, 0), start=st, stop=sp))
                    r = S.emit("tensor", lambda E, l=lhs, p=pair[1], d=d, st=st_, sp=sp_: E.matmul(
                        p[:, :], l, xc_rhs(d, 1), start=st, stop=sp), "PE" if sp_ else None)
                m2_pe[(t, c)] = r

                S.wait("scalar", "PE", m2_pe[(t, c)])
                if c >= 2:
                    S.wait("scalar", "VE", t1_ve[(t, c - 2)])
                ds = (c % 2) * 1024
                for h in range(2):
                    S.emit("scalar", lambda E, p=pair[h], h=h: E.activation(
                        lnscr[:, h * 512:(h + 1) * 512], p[:, :], AF.Exp), drain=(h == 0))
                r = None
                for h in range(2):
                    r = S.emit("scalar", lambda E, h=h, ds=ds: E.activation(
                        dscr[:, ds + h * 512:ds + (h + 1) * 512], lnscr[:, h * 512:(h + 1) * 512],
                        AF.Ln, bias=1.0), "SC", drain=(h == 0))
                sp_sc[(t, c)] = r

                S.wait("vector", "SC", sp_sc[(t, c)])
                if c == 0:
                    S.wait("vector", "PE", bcB_pe)
                r = None
                for h in range(2):
                    r = S.emit("vector", lambda E, h=h, ds=(c % 2) * 1024: E.tensor_tensor(
                        yscr[:, h * 512:(h + 1) * 512], dscr[:, ds + h * 512:ds + (h + 1) * 512],
                        B[6 + h][:, :], OP.mult), "VE" if h == 1 else None, drain=(h == 0))
                t1_ve[(t, c)] = r
                S.emit("vector", lambda E, c=c: E.tensor_scalar(
                    yscr[:, :], yscr[:, :], dms[:, c:c + 1], None, OP.add), drain=True)
                S.emit("vector", lambda E, c=c: E.tensor_tensor(
                    ffns[:, :], yscr[:, :], xcb[:, c * 1024:(c + 1) * 1024], OP.mult), drain=True)
                y2_ve[(t, c)] = S.emit("vector", lambda E, c=c: E.tensor_tensor(
                    szb[:, c * 1024:(c + 1) * 1024], ffns[:, :],
                    szb[:, c * 1024:(c + 1) * 1024], OP.mult), "VE", drain=True)

            pending_stats2 = None
            for c in range(8):
                if c >= 2:
                    S.wait("gpsimd", "PE", m3_pe[(t, c - 2)])
                elif t == 0:
                    S.wait("gpsimd", "SC", sc_kq)
                else:
                    S.wait("gpsimd", "PE", m3_pe[(t - 1, 6 + c)])
                wd_cnt[(t, "w3", c)] = dma(
                    "gpsimd", wm3[:, (c % 2) * 2048:(c % 2 + 1) * 2048], dw3[c, :, :], f"W3S{c % 2}")

                pair = [(B[0], B[1]), (B[2], B[3]), (B[6], B[7])][c % 3]
                S.wait("tensor", f"W3S{c % 2}", wd_cnt[(t, "w3", c)])
                if c >= 3:
                    S.wait("tensor", "VE", resid_ve[(t, c - 3)])
                elif c < 2:
                    S.wait("tensor", "SC", sp_sc[(t, 14 + c)])
                else:
                    S.wait("tensor", "VE", t1_ve[(t, NDI - 1)])
                r = None
                for d in range(NDI):
                    st_, sp_ = (d == 0), (d == NDI - 1)
                    S.wait("tensor", "VE", y2_ve[(t, d)])
                    lhs = wm3[:, (c % 2) * 2048 + d * 128: (c % 2) * 2048 + (d + 1) * 128]
                    S.emit("tensor", lambda E, l=lhs, p=pair[0], d=d, st=st_, sp=sp_: E.matmul(
                        p[:, :], l, sz_rhs(d, 0), start=st, stop=sp))
                    r = S.emit("tensor", lambda E, l=lhs, p=pair[1], d=d, st=st_, sp=sp_: E.matmul(
                        p[:, :], l, sz_rhs(d, 1), start=st, stop=sp), "PE" if sp_ else None)
                m3_pe[(t, c)] = r

                S.wait("vector", "PE", m3_pe[(t, c)])
                r = None
                for h in range(2):
                    r = S.emit("vector", lambda E, c=c, h=h, p=pair[h]: E.tensor_tensor(
                        xbuf[:, c * 1024 + h * 512: c * 1024 + (h + 1) * 512],
                        xbuf[:, c * 1024 + h * 512: c * 1024 + (h + 1) * 512],
                        p[:, :], OP.add), "VE" if h == 1 else None)
                resid_ve[(t, c)] = r

                slot = c % 2
                S.wait("scalar", "VE", resid_ve[(t, c)])
                if c >= 2:
                    S.wait("scalar", "PE", stats2_pe[(t, c - 2)])
                    S.wait("vector", "PE", stats2_pe[(t, c - 2)])
                else:
                    S.wait("scalar", "PE", stats_pe[(t, 6 + c)])
                    S.wait("vector", "PE", stats_pe[(t, 6 + c)])
                cast2v = S.emit("vector", lambda E, c=c, s=slot: E.tensor_copy(
                    xbfs[:, s * 1024:(s + 1) * 1024],
                    xbuf[:, c * 1024:(c + 1) * 1024]), "VE", drain=True)
                cast2 = S.emit("scalar", lambda E, c=c, s=slot: E.activation(
                    sqbfs[:, s * 1024:(s + 1) * 1024],
                    xbuf[:, c * 1024:(c + 1) * 1024], AF.Square), "SC")

                def emit_stats2(c=c, slot=slot, cast2=cast2, cast2v=cast2v, t=t):
                    S.wait("tensor", "SC", cast2)
                    S.wait("tensor", "VE", cast2v)
                    if c == 0:
                        S.wait("tensor", "VE", t1_ve[(t, NDI - 1)])
                    st_, sp_ = (c == 0), (c == 7)
                    S.emit("tensor", lambda E, s=slot, st=st_, sp=sp_: E.matmul(
                        B[4][0:1, :], c1cb[:, 0:1], xbfs[:, s * 1024:s * 1024 + 512], start=st, stop=sp))
                    S.emit("tensor", lambda E, s=slot, st=st_, sp=sp_: E.matmul(
                        B[4][32:33, :], c1cb[:, 0:1], xbfs[:, s * 1024 + 512:(s + 1) * 1024], start=st, stop=sp,
                        tile_position=(0, 32)))
                    S.emit("tensor", lambda E, s=slot, st=st_, sp=sp_: E.matmul(
                        B[5][0:1, :], c1cb[:, 0:1], sqbfs[:, s * 1024:s * 1024 + 512], start=st, stop=sp))
                    stats2_pe[(t, c)] = S.emit("tensor", lambda E, s=slot, st=st_, sp=sp_: E.matmul(
                        B[5][32:33, :], c1cb[:, 0:1], sqbfs[:, s * 1024 + 512:(s + 1) * 1024], start=st, stop=sp,
                        tile_position=(0, 32)), "PE")

                if pending_stats2 is not None:
                    pending_stats2()
                pending_stats2 = emit_stats2

            pending_stats2()
            pending_stats2 = None
            bc2 = ln_finalize(t, 1)
            bcast_pe[(t, 1)] = bc2[1]

            for h in range(2):
                aB = B[6] if h == 0 else B[4]
                dB = B[7] if h == 0 else B[5]
                for cp in range(0, NDM, 2):
                    S.wait("vector", "PE", bc2[h])
                    for c in (cp, cp + 1):
                        S.emit("vector", lambda E, c=c, h=h, aB=aB, r=(c % 2) * 512: E.tensor_tensor(
                            lnscr[:, r:r + 512],
                            xbuf[:, c * 1024 + h * 512: c * 1024 + (h + 1) * 512],
                            aB[:, :], OP.mult), drain=(c == cp))
                    for c in (cp, cp + 1):
                        apply_ve[(t, 1, c, h)] = S.emit(
                            "vector", lambda E, c=c, h=h, dB=dB, r=(c % 2) * 512: E.tensor_tensor(
                                xlnb[:, c * 1024 + h * 512: c * 1024 + (h + 1) * 512],
                                lnscr[:, r:r + 512], dB[:, :], OP.add), "VE", drain=(c == cp))
                    for c in (cp, cp + 1):
                        S.wait("scalar", "VE", apply_ve[(t, 1, c, h)])
                        if t > 0 and c == 0 and h == 0:
                            S.wait("scalar", "PE", m4_pe[(t - 1, 7)])
                        quant_sc[(t, c, h)] = S.emit("scalar", lambda E, c=c, h=h: E.activation(
                            qb[:, c * 1024 + h * 512:c * 1024 + (h + 1) * 512],
                            xlnb[:, c * 1024 + h * 512:c * 1024 + (h + 1) * 512],
                            AF.Copy, scale=scs[:, 0:1]), "SC")

            for f in range(8):
                pair = (B[0], B[1]) if f % 2 == 0 else (B[2], B[3])
                if f >= 2:
                    S.wait("tensor", "SC", m4evac_sc[(t, f - 2)])
                else:
                    S.wait("tensor", "VE", resid_ve[(t, 6 + f)])
                r = None
                if f == 0:
                    for h in range(2):
                        for d in range(NDM):
                            st_, sp_ = (d == 0), (d == NDM - 1)
                            S.wait("tensor", "SC", quant_sc[(t, d, h)])
                            lhs = kqb[:, d * 1024 + f * 128: d * 1024 + (f + 1) * 128]
                            r = S.emit("tensor", lambda E, l=lhs, p=pair[h], d=d, h=h, st=st_, sp=sp_: E.matmul(
                                p[:, :], l, qb[:, d * 1024 + h * 512: d * 1024 + (h + 1) * 512],
                                start=st, stop=sp), "PE" if (sp_ and h == 1) else None)
                else:
                    for d in range(NDM):
                        st_, sp_ = (d == 0), (d == NDM - 1)
                        S.wait("tensor", "SC", quant_sc[(t, d, 0)])
                        lhs = kqb[:, d * 1024 + f * 128: d * 1024 + (f + 1) * 128]
                        S.emit("tensor", lambda E, l=lhs, p=pair[0], d=d, st=st_, sp=sp_: E.matmul(
                            p[:, :], l, qb[:, d * 1024: d * 1024 + 512], start=st, stop=sp))
                        S.wait("tensor", "SC", quant_sc[(t, d, 1)])
                        r = S.emit("tensor", lambda E, l=lhs, p=pair[1], d=d, st=st_, sp=sp_: E.matmul(
                            p[:, :], l, qb[:, d * 1024 + 512: (d + 1) * 1024], start=st, stop=sp),
                            "PE" if sp_ else None)
                m4_pe[(t, f)] = r

                S.wait("scalar", "PE", m4_pe[(t, f)])
                if f >= 1:
                    S.wait("scalar", "VE", final_ve[(t, f - 1)])
                elif t > 0:
                    S.wait("scalar", "VE", final_ve[(t - 1, 7)])
                fslot = f % 2
                r = None
                for h in range(2):
                    r = S.emit("scalar", lambda E, p=pair[h], f=f, h=h: E.activation(
                        ffns[:, h * 512:(h + 1) * 512],
                        p[:, :], AF.Identity, scale=scs[:, 1:2], bias=fbs[:, f:f + 1]),
                        "SC" if h == 1 else None)
                m4evac_sc[(t, f)] = r

                S.wait("vector", "SC", m4evac_sc[(t, f)])
                if f >= 2:
                    S.wait("vector", f"OD{f - 2}", store_od[(t, f - 2)])
                elif t > 0:
                    S.wait("vector", f"OD{6 + f}", store_od[(t - 1, 6 + f)])
                final_ve[(t, f)] = S.emit("vector", lambda E, f=f, fs=fslot: E.tensor_tensor(
                    outst[:, fs * 1024:(fs + 1) * 1024],
                    xbuf[:, f * 1024:(f + 1) * 1024],
                    ffns[:, 0:1024], OP.add), "VE")
                S.wait("sync", "VE", final_ve[(t, f)])
                store_od[(t, f)] = dma(
                    "sync", dout[f * 128:(f + 1) * 128, t * TO:(t + 1) * TO],
                    outst[:, fslot * 1024:(fslot + 1) * 1024], f"OD{f}")

                if t + 1 < NT:
                    ln1_xload(t + 1, f)

            if t + 1 < NT:
                for cc in range(3):
                    w1_dma(t + 1, cc)
                for cc in range(NDM):
                    ln1_cast(t + 1, cc)
                    ln1_statmm(t + 1, cc)
                ln1_tail(t + 1)

        for f in range(8):
            S.wait("sync", f"OD{f}", S.cnt[f"OD{f}"])

        def make_body(items):
            def body(E):
                watermark = {}
                for it in items:
                    if it[0] == "w":
                        _, sem, val = it
                        if val > watermark.get(sem, 0):
                            E.wait_ge(sems[sem], val)
                            watermark[sem] = val
                    else:
                        _, fn, sem, amt, dr = it
                        if dr:
                            E.drain(fusable=True)
                        inst = fn(E)
                        if sem is not None:
                            inst.then_inc(sems[sem], amt)
            return body

        for name in ENGINES:
            if S.q[name]:
                getattr(block, name)(make_body(S.q[name]))

    return nc


def _host_layouts(inputs):
    x = np.asarray(inputs["L0_attn_x"])
    W_in = np.asarray(inputs["L0_attn_W_in"])
    conv_w = np.asarray(inputs["L0_attn_conv_w"])
    conv_b = np.asarray(inputs["L0_attn_conv_b"])
    W_xp = np.asarray(inputs["L0_attn_W_xproj"])
    D = np.asarray(inputs["L0_attn_D"])
    W_out = np.asarray(inputs["L0_attn_W_out"])
    fk = np.asarray(inputs["L0_ffn_kernel"])
    s_in = np.asarray(inputs["L0_ffn_input_scale"])
    s_k = np.asarray(inputs["L0_ffn_kernel_scale"])
    fb = np.asarray(inputs["L0_ffn_bias"])

    W4 = np.ascontiguousarray(W_in.reshape(32, 128, 8, 128).transpose(0, 3, 2, 1))
    w1_all = W4.reshape(32, 128, 1024)
    diag = np.zeros((16, 128, 4, 128), np.float32)
    idx = np.arange(128)
    for c in range(16):
        for k in range(4):
            diag[c, idx, k, idx] = conv_w[c * 128 + idx, k]
    w1a = np.ascontiguousarray(
        np.concatenate([w1_all[:16], diag.reshape(16, 128, 512)], axis=2))
    w1b = np.ascontiguousarray(w1_all[16:])
    w2 = np.ascontiguousarray(
        W_xp[32:].reshape(16, 128, 16, 128).transpose(0, 3, 2, 1).reshape(16, 128, 2048))
    wbc = np.ascontiguousarray(
        np.stack([W_xp[:32, dd * 128:(dd + 1) * 128].T for dd in range(16)], axis=1).reshape(128, 512))
    w3 = np.ascontiguousarray(
        W_out.reshape(8, 128, 16, 128).transpose(0, 3, 2, 1).reshape(8, 128, 2048))
    fk_t = np.ascontiguousarray(fk.reshape(8, 128, 1024))
    cb = np.ascontiguousarray(conv_b.reshape(16, 128).T)
    dm = np.ascontiguousarray(D.reshape(16, 128).T)
    fbm = np.ascontiguousarray(fb.reshape(8, 128).T)
    scales = np.array([[float(s_in[0]), float(s_k[0])]], np.float32)
    shared = {
        "w1a": w1a, "w1b": w1b, "w2": w2, "wbc": wbc, "w3": w3, "fk": fk_t,
        "cb": cb, "dmat": dm, "fbias": fbm, "scales": scales,
        "onesr": np.ones((1, 128), np.float32),
        "negr": -np.ones((1, 128), np.float32),
        "onescf": np.ones((128, 1), np.float32),
        "onescb": np.ones((128, 1), BF),
        "c1cb": np.full((128, 1), 1.0 / DM, BF),
    }
    in_maps = []
    for b in range(NCORES):
        m = dict(shared)
        m["x"] = np.ascontiguousarray(x[b].T)
        in_maps.append(m)
    return in_maps


def kernel(**inputs):
    in_maps = _host_layouts(inputs)
    nc = build_nc()
    res = run_bass_kernel_spmd(nc, in_maps, core_ids=list(range(NCORES)))
    out = np.stack([np.asarray(res.results[b]["out"]).T for b in range(NCORES)], axis=0)
    return np.ascontiguousarray(out.astype(np.float32))


# version 52 (final)
# speedup vs baseline: 1.0165x; 1.0165x over previous
import numpy as np
import ml_dtypes
import concourse.bass as bass
import concourse.mybir as mybir
from concourse.bass_utils import run_bass_kernel_spmd

F32 = mybir.dt.float32
BF16 = mybir.dt.bfloat16
FP8 = mybir.dt.float8e4
AF = mybir.ActivationFunctionType
OP = mybir.AluOpType
BF = ml_dtypes.bfloat16

NCORES = 8
L = 2048
DM = 1024
DI = 2048
NDM = DM // 128
NDI = DI // 128
NT = 2
TO = 1024
EPS = 1e-5

ENGINES = ("sync", "gpsimd", "tensor", "scalar", "vector")

PREV3 = {0: 15, 1: 13, 2: 14}


class Sched:
    def __init__(self):
        self.q = {e: [] for e in ENGINES}
        self.cnt = {}

    def sem(self, name):
        self.cnt[name] = 0

    def wait(self, eng, sem, val):
        if val and val > 0:
            self.q[eng].append(("w", sem, int(val)))

    def emit(self, eng, fn, sem=None, amt=1, drain=False):
        self.q[eng].append(("o", fn, sem, amt, drain))
        if sem is not None:
            self.cnt[sem] += amt
            return self.cnt[sem]
        return None


def build_nc():
    nc = bass.Bass("TRN2", target_bir_lowering=False)

    dx = nc.dram_tensor("x", [DM, L], F32, kind="ExternalInput").ap()
    dw1a = nc.dram_tensor("w1a", [16, 128, 1536], F32, kind="ExternalInput").ap()
    dw1b = nc.dram_tensor("w1b", [16, 128, 1024], F32, kind="ExternalInput").ap()
    dw2 = nc.dram_tensor("w2", [16, 128, 2048], F32, kind="ExternalInput").ap()
    dwbc = nc.dram_tensor("wbc", [128, 512], F32, kind="ExternalInput").ap()
    dw3 = nc.dram_tensor("w3", [8, 128, 2048], F32, kind="ExternalInput").ap()
    dfk = nc.dram_tensor("fk", [8, 128, 1024], BF16, kind="ExternalInput").ap()
    dcb = nc.dram_tensor("cb", [128, 16], F32, kind="ExternalInput").ap()
    ddm = nc.dram_tensor("dmat", [128, 16], F32, kind="ExternalInput").ap()
    dfb = nc.dram_tensor("fbias", [128, 8], BF16, kind="ExternalInput").ap()
    dsc = nc.dram_tensor("scales", [1, 2], F32, kind="ExternalInput").ap()
    donr = nc.dram_tensor("onesr", [1, 128], F32, kind="ExternalInput").ap()
    doncf = nc.dram_tensor("onescf", [128, 1], F32, kind="ExternalInput").ap()
    dc1cb = nc.dram_tensor("c1cb", [128, 1], BF16, kind="ExternalInput").ap()
    dout = nc.dram_tensor("out", [DM, L], F32, kind="ExternalOutput").ap()

    from contextlib import ExitStack
    with ExitStack() as st:
        en = st.enter_context
        xbuf = en(nc.sbuf_tensor("xbuf", [128, 8192], F32))
        xlnb = en(nc.sbuf_tensor("xlnb", [128, 8192], BF16))
        xcb = en(nc.sbuf_tensor("xcb", [128, 16384], BF16))
        szb = en(nc.sbuf_tensor("szb", [128, 16384], BF16))
        xbs = en(nc.sbuf_tensor("xbs", [128, 4 * 1028], BF16))
        halo = en(nc.sbuf_tensor("halo", [128, 48], BF16))
        qb = en(nc.sbuf_tensor("qb", [128, 8192], FP8))
        kqb = en(nc.sbuf_tensor("kqb", [128, 8192], FP8))
        wm1 = en(nc.sbuf_tensor("wm1", [128, 4 * 1536], BF16))
        wm2 = en(nc.sbuf_tensor("wm2", [128, 3 * 2048], BF16))
        wm3 = en(nc.sbuf_tensor("wm3", [128, 2 * 2048], BF16))
        wbcb = en(nc.sbuf_tensor("wbcb", [128, 512], BF16))
        dscr = en(nc.sbuf_tensor("dscr", [128, 2048], F32))
        yscr = en(nc.sbuf_tensor("yscr", [128, 1024], F32))
        lnscr = en(nc.sbuf_tensor("lnscr", [128, 1024], F32))
        xbfs = en(nc.sbuf_tensor("xbfs", [128, 2048], BF16))
        sqbfs = en(nc.sbuf_tensor("sqbfs", [128, 2048], BF16))
        outst = en(nc.sbuf_tensor("outst", [128, 2048], F32))
        statb = en(nc.sbuf_tensor("statb", [128, 1024], F32))
        ffns = en(nc.sbuf_tensor("ffns", [128, 1024], BF16))
        onr = en(nc.sbuf_tensor("onr", [33, 128], F32))
        oncf = en(nc.sbuf_tensor("oncf", [128, 4], F32))
        c1cb = en(nc.sbuf_tensor("c1cs", [128, 1], BF16))
        cbs = en(nc.sbuf_tensor("cbs", [128, 16], F32))
        dms = en(nc.sbuf_tensor("dms", [128, 16], F32))
        fbs = en(nc.sbuf_tensor("fbs", [128, 8], BF16))
        B = [en(nc.psum_tensor(f"B{i}", [128, 512], F32)) for i in range(8)]
        sems = {}
        sem_names = (["CD", "FK", "WB", "PE", "SC", "VE", "VCH", "SCH"]
                     + [f"W1S{i}" for i in range(4)]
                     + [f"W2S{i}" for i in range(3)]
                     + [f"W3S{i}" for i in range(2)]
                     + [f"XL{i}" for i in range(8)]
                     + [f"OD{i}" for i in range(8)])
        for s in sem_names:
            sems[s] = en(nc.semaphore(s))
        block = en(nc.Block())

        S = Sched()
        for s in sems:
            S.sem(s)

        wd_cnt = {}
        m1_pe, conv_pe, m2_pe, m3_pe, m4_pe = {}, {}, {}, {}, {}
        evac_sc, silu_sc, zevac_sc, sp_sc, m4evac_sc, quant_sc = {}, {}, {}, {}, {}, {}
        apply_ve, t1_ve, y2_ve, resid_ve, final_ve = {}, {}, {}, {}, {}
        hrest_ve, hsave_ve = {}, {}
        store_od, xload_xd, cast_sc, castv_ve = {}, {}, {}, {}
        stats_pe, stats2_pe, bcast_pe = {}, {}, {}

        def dma(eng, dst, src, sem):
            return S.emit(eng, lambda E, d=dst, s_=src: E.dma_start(out=d, in_=s_), sem, 16)

        dma("sync", c1cb[:, :], dc1cb, "CD")
        xload_xd[(0, 0)] = dma(
            "sync", xbuf[:, 0:1024], dx[0:128, 0:TO], "XL0")
        dma("sync", onr[0:1, :], donr, "CD")
        dma("sync", onr[32:33, :], donr, "CD")
        dma("sync", oncf[:, 0:1], doncf, "CD")
        dma("sync", cbs[:, :], dcb, "CD")
        dma("sync", dms[:, :], ddm, "CD")
        dma("sync", fbs[:, :], dfb, "CD")
        xd_consts = dma("sync", statb[0:1,0:2], dsc, "CD")
        for c in range(1, NDM):
            xload_xd[(0, c)] = dma(
                "sync", xbuf[:, c * 1024:(c + 1) * 1024],
                dx[c * 128:(c + 1) * 128, 0:TO], f"XL{c}")
        for i in range(8):
            area, off = (wm2, (i % 4) * 1024) if i < 4 else (wm3, (i - 4) * 1024)
            xd_setup = dma("sync", area[:, off:off + 1024], dfk[i, :, :], "FK")

        S.wait("vector", "CD", xd_consts)
        S.emit("vector", lambda E: E.reciprocal(statb[0:1,2:3], statb[0:1,1:2]))
        S.emit("vector", lambda E: E.reciprocal(statb[0:1,3:4], statb[0:1,0:1]))
        ve_rows = S.emit("vector", lambda E: E.tensor_tensor(
            statb[0:1,4:5], statb[0:1,0:1], statb[0:1,1:2], OP.mult), "VE")
        S.wait("tensor", "VE", ve_rows)
        S.emit("tensor", lambda E: E.matmul(B[0][:, 0:1], onr[0:1, :], statb[0:1,3:4], start=True, stop=True))
        S.emit("tensor", lambda E: E.matmul(B[0][:, 1:2], onr[0:1, :], statb[0:1,4:5], start=True, stop=True))
        pe_sc = S.emit("tensor", lambda E: E.matmul(B[0][:, 2:3], onr[0:1, :], statb[0:1,2:3], start=True, stop=True), "PE")
        S.wait("vector", "PE", pe_sc)
        S.emit("vector", lambda E: E.tensor_copy(oncf[:, 1:4], B[0][:, 0:3]), "VE")
        S.emit("vector", lambda E: E.memset(halo[:, :], 0.0))
        ve_setup = S.emit("vector", lambda E: E.memset(xbs[:, :], 0.0), "VE")
        wd_wbc = dma("gpsimd", wbcb[:, :], dwbc, "WB")
        sc_kq = 0

        def m1_rhs(d, h):
            return xlnb[:, d * 1024 + h * 512: d * 1024 + (h + 1) * 512]

        def xc_rhs(d, h):
            return xcb[:, d * 1024 + h * 512: d * 1024 + (h + 1) * 512]

        def sz_rhs(d, h):
            return szb[:, d * 1024 + h * 512: d * 1024 + (h + 1) * 512]

        def ln_finalize(t, ln):
            sp_dict = stats_pe if ln == 0 else stats2_pe
            S.wait("vector", "PE", sp_dict[(t, NDM - 1)])
            S.wait("scalar", "PE", sp_dict[(t, NDM - 1)])

            def warm(n, gate=None):
                if gate is not None:
                    S.wait("tensor", gate[0], gate[1])
                for _ in range(n):
                    S.emit("tensor", lambda E: E.matmul(
                        B[6][64:65, :], c1cb[:, 0:1], xbfs[:, 0:512],
                        start=True, stop=True, skip_group_check=True,
                        tile_position=(0, 64)))

            bc = {}
            for i, hr in enumerate((0, 32)):
                sq_m = S.emit("scalar", lambda E, hr=hr: E.activation(
                    statb[hr:hr + 1, 0:512], B[4][hr:hr + 1, :], AF.Square), "SC")
                S.emit("vector", lambda E, hr=hr: E.tensor_scalar(
                    statb[hr:hr + 1, 512:1024], B[5][hr:hr + 1, :], EPS, None, OP.add))
                S.wait("vector", "SC", sq_m)
                v_var = S.emit("vector", lambda E, hr=hr: E.tensor_tensor(
                    statb[hr:hr + 1, 0:512], statb[hr:hr + 1, 512:1024],
                    statb[hr:hr + 1, 0:512], OP.subtract), "VE", drain=True)
                if i == 0:
                    warm(3, ("SC", sq_m))
                S.wait("scalar", "VE", v_var)
                S.emit("scalar", lambda E, hr=hr: E.activation(
                    statb[hr:hr + 1, 512:1024], statb[hr:hr + 1, 0:512], AF.Ln),
                    drain=True)
                s_a = S.emit("scalar", lambda E, hr=hr: E.activation(
                    statb[hr:hr + 1, 0:512], statb[hr:hr + 1, 512:1024], AF.Exp, scale=-0.5),
                    "SC", drain=True)
                S.wait("vector", "SC", s_a)
                v_d = S.emit("vector", lambda E, hr=hr: E.tensor_tensor(
                    statb[hr:hr + 1, 512:1024], B[4][hr:hr + 1, :],
                    statb[hr:hr + 1, 0:512], OP.mult), "VE", drain=True)
                S.wait("tensor", "VE", v_d)
                if i == 0:
                    S.emit("tensor", lambda E: E.matmul(
                        B[6][:, :], onr[0:1, :], statb[0:1, 0:512], start=True, stop=True))
                    bc[0] = S.emit("tensor", lambda E: E.matmul(
                        B[7][:, :], onr[0:1, :], statb[0:1, 512:1024], start=True, stop=True), "PE")
                else:
                    S.emit("tensor", lambda E: E.matmul(
                        B[4][:, :], onr[32:33, :], statb[32:33, 0:512], start=True, stop=True))
                    bc[1] = S.emit("tensor", lambda E: E.matmul(
                        B[5][:, :], onr[32:33, :], statb[32:33, 512:1024], start=True, stop=True), "PE")
            return bc

        def w1_dma(t, c):
            slot = c % 4
            if c >= 4:
                cc = c - 4
                S.wait("gpsimd", "PE", conv_pe[(t, cc)] if cc < 16 else m1_pe[(t, cc)])
            elif t > 0:
                S.wait("gpsimd", "PE", m1_pe[(t - 1, 28 + c)])
            if c < 16:
                src = dw1a[c, :, :]
                dst = wm1[:, slot * 1536: slot * 1536 + 1536]
            else:
                src = dw1b[c - 16, :, :]
                dst = wm1[:, slot * 1536: slot * 1536 + 1024]
            wd_cnt[(t, "w1", c)] = dma("gpsimd", dst, src, f"W1S{slot}")

        def ln1_xload(t, c):
            S.wait("sync", "VE", final_ve[(t - 1, c)])
            xload_xd[(t, c)] = dma(
                "sync", xbuf[:, c * 1024:(c + 1) * 1024],
                dx[c * 128:(c + 1) * 128, t * TO:(t + 1) * TO], f"XL{c}")

        def ln1_cast(t, c):
            slot = c % 2
            S.wait("scalar", f"XL{c}", xload_xd[(t, c)])
            S.wait("vector", f"XL{c}", xload_xd[(t, c)])
            if c >= 2:
                S.wait("scalar", "PE", stats_pe[(t, c - 2)])
                S.wait("vector", "PE", stats_pe[(t, c - 2)])
            elif t > 0:
                S.wait("scalar", "PE", stats2_pe[(t - 1, 6 + c)])
                S.wait("vector", "PE", stats2_pe[(t - 1, 6 + c)])
            castv_ve[(t, c)] = S.emit("vector", lambda E, c=c, s=slot: E.tensor_copy(
                xbfs[:, s * 1024:(s + 1) * 1024],
                xbuf[:, c * 1024:(c + 1) * 1024]), "VE")
            cast_sc[(t, c)] = S.emit("scalar", lambda E, c=c, s=slot: E.activation(
                sqbfs[:, s * 1024:(s + 1) * 1024],
                xbuf[:, c * 1024:(c + 1) * 1024], AF.Square), "SC")

        def ln1_statmm(t, c):
            slot = c % 2
            S.wait("tensor", "SC", cast_sc[(t, c)])
            S.wait("tensor", "VE", castv_ve[(t, c)])
            S.wait("tensor", "CD", xd_consts)
            if t > 0 and c == 0:
                S.wait("tensor", "VE", apply_ve[(t - 1, 1, NDM - 1, 1)])
            st_, sp_ = (c == 0), (c == NDM - 1)
            S.emit("tensor", lambda E, s=slot, st=st_, sp=sp_: E.matmul(
                B[4][0:1, :], c1cb[:, 0:1], xbfs[:, s * 1024:s * 1024 + 512], start=st, stop=sp))
            S.emit("tensor", lambda E, s=slot, st=st_, sp=sp_: E.matmul(
                B[4][32:33, :], c1cb[:, 0:1], xbfs[:, s * 1024 + 512:(s + 1) * 1024], start=st, stop=sp,
                tile_position=(0, 32)))
            S.emit("tensor", lambda E, s=slot, st=st_, sp=sp_: E.matmul(
                B[5][0:1, :], c1cb[:, 0:1], sqbfs[:, s * 1024:s * 1024 + 512], start=st, stop=sp))
            stats_pe[(t, c)] = S.emit("tensor", lambda E, s=slot, st=st_, sp=sp_: E.matmul(
                B[5][32:33, :], c1cb[:, 0:1], sqbfs[:, s * 1024 + 512:(s + 1) * 1024], start=st, stop=sp,
                tile_position=(0, 32)), "PE")

        def ln1_tail(t):
            bc = ln_finalize(t, 0)
            bcast_pe[(t, 0)] = bc[1]
            for h in range(2):
                aB = B[6] if h == 0 else B[4]
                dB = B[7] if h == 0 else B[5]
                for cp in range(0, NDM, 2):
                    S.wait("vector", "PE", bc[h])
                    if t > 0 and cp == 0 and h == 0:
                        S.wait("vector", "SC", quant_sc[(t - 1, NDM - 1, 1)])
                    for c in (cp, cp + 1):
                        S.emit("vector", lambda E, c=c, h=h, aB=aB, r=(c % 2) * 512: E.tensor_tensor(
                            lnscr[:, r:r + 512],
                            xbuf[:, c * 1024 + h * 512: c * 1024 + (h + 1) * 512],
                            aB[:, :], OP.mult), drain=(c == cp))
                    for c in (cp, cp + 1):
                        apply_ve[(t, 0, c, h)] = S.emit(
                            "vector", lambda E, c=c, h=h, dB=dB, r=(c % 2) * 512: E.tensor_tensor(
                                xlnb[:, c * 1024 + h * 512: c * 1024 + (h + 1) * 512],
                                lnscr[:, r:r + 512], dB[:, :], OP.subtract), "VE", drain=(c == cp))

        for t in range(NT):
            if t == 0:
                for c in range(4):
                    w1_dma(0, c)
                for c in range(NDM):
                    ln1_cast(0, c)
                    ln1_statmm(0, c)
                ln1_tail(0)

            bcproj_pe = v_prod = sc_cm = None
            bcred_pe = v_bcrows = bcB_pe = None
            def emit_bc_part1():
                nonlocal bcproj_pe, v_prod, sc_cm
                S.wait("tensor", "SC", silu_sc[(t, 15)])
                S.wait("tensor", "WB", wd_wbc)
                r = None
                for d in range(NDI):
                    st_, sp_ = (d == 0), (d == NDI - 1)
                    S.emit("tensor", lambda E, d=d, st=st_, sp=sp_: E.matmul(
                        B[4][0:16, :], wbcb[:, d * 32:d * 32 + 16], xc_rhs(d, 0), start=st, stop=sp,
                        skip_group_check=True))
                    S.emit("tensor", lambda E, d=d, st=st_, sp=sp_: E.matmul(
                        B[4][32:48, :], wbcb[:, d * 32:d * 32 + 16], xc_rhs(d, 1), start=st, stop=sp,
                        tile_position=(0, 32), skip_group_check=True))
                    S.emit("tensor", lambda E, d=d, st=st_, sp=sp_: E.matmul(
                        B[5][0:16, :], wbcb[:, d * 32 + 16:(d + 1) * 32], xc_rhs(d, 0), start=st, stop=sp,
                        skip_group_check=True))
                    r = S.emit("tensor", lambda E, d=d, st=st_, sp=sp_: E.matmul(
                        B[5][32:48, :], wbcb[:, d * 32 + 16:(d + 1) * 32], xc_rhs(d, 1), start=st, stop=sp,
                        tile_position=(0, 32), skip_group_check=True), "PE" if sp_ else None)
                bcproj_pe = r
                S.wait("scalar", "PE", bcproj_pe)
                sc_cm = S.emit("scalar", lambda E: E.activation(
                    yscr[0:48, 512:1024], B[5][0:48, :], AF.Copy), "SC")
                S.wait("vector", "SC", sc_cm)
                S.emit("vector", lambda E: E.tensor_tensor(
                    yscr[0:16, 0:512], B[4][0:16, :], yscr[0:16, 512:1024], OP.mult), drain=True)
                v_prod = S.emit("vector", lambda E: E.tensor_tensor(
                    yscr[32:48, 0:512], B[4][32:48, :], yscr[32:48, 512:1024], OP.mult), "VE")
            def emit_bc_part2():
                nonlocal bcred_pe, v_bcrows, bcB_pe
                S.wait("tensor", "VE", v_prod)
                S.emit("tensor", lambda E: E.matmul(
                    B[6][0:1, 0:512], oncf[0:16, 0:1], yscr[0:16, 0:512], start=True, stop=True))
                bcred_pe = S.emit("tensor", lambda E: E.matmul(
                    B[6][32:33, 0:512], oncf[32:48, 0:1], yscr[32:48, 0:512], start=True, stop=True,
                    tile_position=(32, 32)), "PE")
                S.wait("scalar", "PE", bcred_pe)
                S.emit("scalar", lambda E: E.activation(lnscr[0:1, 0:512], B[6][0:1, :], AF.Copy))
                v_bcrows = S.emit("scalar", lambda E: E.activation(
                    lnscr[32:33, 0:512], B[6][32:33, :], AF.Copy), "SC")
                S.wait("tensor", "SC", v_bcrows)
                S.emit("tensor", lambda E: E.matmul(
                    B[6][:, :], onr[0:1, :], lnscr[0:1, 0:512], start=True, stop=True))
                bcB_pe = S.emit("tensor", lambda E: E.matmul(
                    B[7][:, :], onr[32:33, :], lnscr[32:33, 0:512], start=True, stop=True), "PE")

            pending_conv = []
            for c in range(32):
                slot = c % 4
                pair = (B[0], B[1]) if c % 2 == 0 else (B[2], B[3])
                S.wait("tensor", f"W1S{c % 4}", wd_cnt[(t, "w1", c)])
                if c >= 2:
                    S.wait("tensor", "SC", evac_sc.get((t, c - 2)) or zevac_sc.get((t, c - 2)))
                elif t > 0:
                    S.wait("tensor", "SC", m4evac_sc[(t - 1, 6 + c)])
                r = None
                if c == 0:
                    for h in range(2):
                        for d in range(NDM):
                            st_, sp_ = (d == 0), (d == NDM - 1)
                            S.wait("tensor", "VE", apply_ve[(t, 0, d, h)])
                            lhs = wm1[:, slot * 1536 + d * 128: slot * 1536 + (d + 1) * 128]
                            r = S.emit("tensor", lambda E, l=lhs, p=pair[h], d=d, h=h, st=st_, sp=sp_: E.matmul(
                                p[:, :], l, m1_rhs(d, h), start=st, stop=sp),
                                "PE" if (sp_ and h == 1) else None)
                else:
                    for d in range(NDM):
                        st_, sp_ = (d == 0), (d == NDM - 1)
                        S.wait("tensor", "VE", apply_ve[(t, 0, d, 0)])
                        lhs = wm1[:, slot * 1536 + d * 128: slot * 1536 + (d + 1) * 128]
                        S.emit("tensor", lambda E, l=lhs, p=pair[0], d=d, st=st_, sp=sp_: E.matmul(
                            p[:, :], l, m1_rhs(d, 0), start=st, stop=sp))
                        S.wait("tensor", "VE", apply_ve[(t, 0, d, 1)])
                        r = S.emit("tensor", lambda E, l=lhs, p=pair[1], d=d, st=st_, sp=sp_: E.matmul(
                            p[:, :], l, m1_rhs(d, 1), start=st, stop=sp), "PE" if sp_ else None)
                m1_pe[(t, c)] = r

                S.wait("scalar", "PE", m1_pe[(t, c)])
                if c < 16:
                    base = (c % 4) * 1028
                    if c >= 4:
                        S.wait("scalar", "PE", conv_pe[(t, c - 4)])
                        S.wait("scalar", "VE", hsave_ve[(t, c - 4)])
                    elif t > 0:
                        S.wait("scalar", "PE", conv_pe[(t - 1, 12 + c)])
                        S.wait("scalar", "VE", hsave_ve[(t - 1, 12 + c)])
                    S.emit("scalar", lambda E, p=pair[0], b=base: E.activation(
                        xbs[:, b + 3: b + 515], p[:, :], AF.Copy))
                    evac_sc[(t, c)] = S.emit("scalar", lambda E, p=pair[1], b=base: E.activation(
                        xbs[:, b + 515: b + 1027], p[:, :], AF.Copy), "SC")
                    if c >= 4:
                        S.wait("vector", "PE", conv_pe[(t, c - 4)])
                    elif t > 0:
                        S.wait("vector", "PE", conv_pe[(t - 1, 12 + c)])
                    hrest_ve[(t, c)] = S.emit("vector", lambda E, c=c, b=base: E.tensor_copy(
                        xbs[:, b: b + 3], halo[:, c * 3:(c + 1) * 3]), "VE")
                    S.wait("vector", "SC", evac_sc[(t, c)])
                    hsave_ve[(t, c)] = S.emit("vector", lambda E, c=c, b=base: E.tensor_copy(
                        halo[:, c * 3:(c + 1) * 3], xbs[:, b + 1024: b + 1027]), "VE", drain=True)

                    def emit_conv(c=c, base=base, slot=slot, t=t):
                        cpair = (B[4], B[5]) if c % 2 == 0 else (B[6], B[7])
                        S.wait("tensor", "SC", evac_sc[(t, c)])
                        S.wait("tensor", "VE", hrest_ve[(t, c)])
                        if c >= 2:
                            S.wait("tensor", "SC", silu_sc[(t, c - 2)])
                        r = None
                        for k in range(4):
                            st_, sp_ = (k == 0), (k == 3)
                            lhs = wm1[:, slot * 1536 + 1024 + k * 128: slot * 1536 + 1024 + (k + 1) * 128]
                            S.emit("tensor", lambda E, l=lhs, p=cpair[0], b=base, k=k, st=st_, sp=sp_: E.matmul(
                                p[:, :], l, xbs[:, b + k: b + k + 512], start=st, stop=sp))
                            r = S.emit("tensor", lambda E, l=lhs, p=cpair[1], b=base, k=k, st=st_, sp=sp_: E.matmul(
                                p[:, :], l, xbs[:, b + 512 + k: b + 1024 + k], start=st, stop=sp),
                                "PE" if sp_ else None)
                        conv_pe[(t, c)] = r
                        S.wait("scalar", "PE", conv_pe[(t, c)])
                        if t > 0 and c == 0:
                            S.wait("scalar", "PE", m2_pe[(t - 1, NDI - 1)])
                        S.emit("scalar", lambda E, p=cpair[0], c=c: E.activation(
                            xcb[:, c * 1024: c * 1024 + 512], p[:, :], AF.Silu, bias=cbs[:, c:c + 1]))
                        silu_sc[(t, c)] = S.emit("scalar", lambda E, p=cpair[1], c=c: E.activation(
                            xcb[:, c * 1024 + 512: (c + 1) * 1024], p[:, :], AF.Silu, bias=cbs[:, c:c + 1]), "SC")
                        if t == 0 and c % 2 == 1:
                            i = c // 2
                            area, off = (wm2, (i % 4) * 1024) if i < 4 else (wm3, (i - 4) * 1024)
                            S.wait("scalar", "FK", xd_setup)
                            S.wait("scalar", "VE", ve_setup)
                            nonlocal sc_kq
                            sc_kq = S.emit("scalar", lambda E, a=area, o=off, i=i: E.activation(
                                kqb[:, i * 1024:(i + 1) * 1024], a[:, o:o + 1024], AF.Copy,
                                scale=oncf[:, 3:4]), "SC")
                        if c + 4 < 32:
                            w1_dma(t, c + 4)

                    pending_conv.append(emit_conv)
                    if len(pending_conv) > 1:
                        pending_conv.pop(0)()
                else:
                    zc = c - 16
                    if t > 0 and zc == 0:
                        S.wait("scalar", "PE", m3_pe[(t - 1, 7)])
                    S.emit("scalar", lambda E, p=pair[0], zc=zc: E.activation(
                        szb[:, zc * 1024: zc * 1024 + 512], p[:, :], AF.Silu))
                    zevac_sc[(t, c)] = S.emit("scalar", lambda E, p=pair[1], zc=zc: E.activation(
                        szb[:, zc * 1024 + 512: (zc + 1) * 1024], p[:, :], AF.Silu), "SC")
                    while pending_conv:
                        pending_conv.pop(0)()
                    if c == 16:
                        emit_bc_part1()
                    if c + 4 < 32:
                        w1_dma(t, c + 4)

            emit_bc_part2()

            for c in range(NDI):
                if c >= 3:
                    S.wait("gpsimd", "PE", m2_pe[(t, c - 3)])
                elif t == 0 and c == 0:
                    S.wait("gpsimd", "SC", sc_kq)
                elif t > 0:
                    S.wait("gpsimd", "PE", m2_pe[(t - 1, 13 + c)])
                wd_cnt[(t, "w2", c)] = dma(
                    "gpsimd", wm2[:, (c % 3) * 2048:(c % 3 + 1) * 2048], dw2[c, :, :], f"W2S{c % 3}")

                pair = (B[0], B[1]) if c % 2 == 0 else (B[2], B[3])
                S.wait("tensor", f"W2S{c % 3}", wd_cnt[(t, "w2", c)])
                if c >= 2:
                    S.wait("tensor", "SC", sp_sc[(t, c - 2)])
                else:
                    S.wait("tensor", "SC", zevac_sc[(t, 30 + c)])
                r = None
                for d in range(NDI):
                    st_, sp_ = (d == 0), (d == NDI - 1)
                    lhs = wm2[:, (c % 3) * 2048 + d * 128: (c % 3) * 2048 + (d + 1) * 128]
                    S.emit("tensor", lambda E, l=lhs, p=pair[0], d=d, st=st_, sp=sp_: E.matmul(
                        p[:, :], l, xc_rhs(d, 0), start=st, stop=sp))
                    r = S.emit("tensor", lambda E, l=lhs, p=pair[1], d=d, st=st_, sp=sp_: E.matmul(
                        p[:, :], l, xc_rhs(d, 1), start=st, stop=sp), "PE" if sp_ else None)
                m2_pe[(t, c)] = r

                S.wait("scalar", "PE", m2_pe[(t, c)])
                if c >= 2:
                    S.wait("scalar", "VE", t1_ve[(t, c - 2)])
                ds = (c % 2) * 1024
                for h in range(2):
                    S.emit("scalar", lambda E, p=pair[h], h=h: E.activation(
                        lnscr[:, h * 512:(h + 1) * 512], p[:, :], AF.Exp), drain=(h == 0))
                r = None
                for h in range(2):
                    r = S.emit("scalar", lambda E, h=h, ds=ds: E.activation(
                        dscr[:, ds + h * 512:ds + (h + 1) * 512], lnscr[:, h * 512:(h + 1) * 512],
                        AF.Ln, bias=1.0), "SC", drain=(h == 0))
                sp_sc[(t, c)] = r

                S.wait("vector", "SC", sp_sc[(t, c)])
                if c == 0:
                    S.wait("vector", "PE", bcB_pe)
                r = None
                for h in range(2):
                    r = S.emit("vector", lambda E, h=h, ds=(c % 2) * 1024: E.tensor_tensor(
                        yscr[:, h * 512:(h + 1) * 512], dscr[:, ds + h * 512:ds + (h + 1) * 512],
                        B[6 + h][:, :], OP.mult), "VE" if h == 1 else None, drain=(h == 0))
                t1_ve[(t, c)] = r
                S.emit("vector", lambda E, c=c: E.tensor_scalar(
                    yscr[:, :], yscr[:, :], dms[:, c:c + 1], None, OP.add), drain=True)
                S.emit("vector", lambda E, c=c: E.tensor_tensor(
                    ffns[:, :], yscr[:, :], xcb[:, c * 1024:(c + 1) * 1024], OP.mult), drain=True)
                y2_ve[(t, c)] = S.emit("vector", lambda E, c=c: E.tensor_tensor(
                    szb[:, c * 1024:(c + 1) * 1024], ffns[:, :],
                    szb[:, c * 1024:(c + 1) * 1024], OP.mult), "VE", drain=True)

            pending_stats2 = None
            for c in range(8):
                if c >= 2:
                    S.wait("gpsimd", "PE", m3_pe[(t, c - 2)])
                elif t == 0:
                    S.wait("gpsimd", "SC", sc_kq)
                else:
                    S.wait("gpsimd", "PE", m3_pe[(t - 1, 6 + c)])
                wd_cnt[(t, "w3", c)] = dma(
                    "gpsimd", wm3[:, (c % 2) * 2048:(c % 2 + 1) * 2048], dw3[c, :, :], f"W3S{c % 2}")

                pair = (B[0], B[1]) if c % 2 == 0 else (B[2], B[3])
                S.wait("tensor", f"W3S{c % 2}", wd_cnt[(t, "w3", c)])
                if c >= 2:
                    S.wait("tensor", "VE", resid_ve[(t, c - 2)])
                else:
                    S.wait("tensor", "SC", sp_sc[(t, 14 + c)])
                r = None
                for d in range(NDI):
                    st_, sp_ = (d == 0), (d == NDI - 1)
                    S.wait("tensor", "VE", y2_ve[(t, d)])
                    lhs = wm3[:, (c % 2) * 2048 + d * 128: (c % 2) * 2048 + (d + 1) * 128]
                    S.emit("tensor", lambda E, l=lhs, p=pair[0], d=d, st=st_, sp=sp_: E.matmul(
                        p[:, :], l, sz_rhs(d, 0), start=st, stop=sp))
                    r = S.emit("tensor", lambda E, l=lhs, p=pair[1], d=d, st=st_, sp=sp_: E.matmul(
                        p[:, :], l, sz_rhs(d, 1), start=st, stop=sp), "PE" if sp_ else None)
                m3_pe[(t, c)] = r

                S.wait("vector", "PE", m3_pe[(t, c)])
                r = None
                for h in range(2):
                    r = S.emit("vector", lambda E, c=c, h=h, p=pair[h]: E.tensor_tensor(
                        xbuf[:, c * 1024 + h * 512: c * 1024 + (h + 1) * 512],
                        xbuf[:, c * 1024 + h * 512: c * 1024 + (h + 1) * 512],
                        p[:, :], OP.add), "VE" if h == 1 else None)
                resid_ve[(t, c)] = r

                slot = c % 2
                S.wait("scalar", "VE", resid_ve[(t, c)])
                if c >= 2:
                    S.wait("scalar", "PE", stats2_pe[(t, c - 2)])
                    S.wait("vector", "PE", stats2_pe[(t, c - 2)])
                else:
                    S.wait("scalar", "PE", stats_pe[(t, 6 + c)])
                    S.wait("vector", "PE", stats_pe[(t, 6 + c)])
                cast2v = S.emit("vector", lambda E, c=c, s=slot: E.tensor_copy(
                    xbfs[:, s * 1024:(s + 1) * 1024],
                    xbuf[:, c * 1024:(c + 1) * 1024]), "VE", drain=True)
                cast2 = S.emit("scalar", lambda E, c=c, s=slot: E.activation(
                    sqbfs[:, s * 1024:(s + 1) * 1024],
                    xbuf[:, c * 1024:(c + 1) * 1024], AF.Square), "SC")

                def emit_stats2(c=c, slot=slot, cast2=cast2, cast2v=cast2v, t=t):
                    S.wait("tensor", "SC", cast2)
                    S.wait("tensor", "VE", cast2v)
                    if c == 0:
                        S.wait("tensor", "VE", t1_ve[(t, NDI - 1)])
                    st_, sp_ = (c == 0), (c == 7)
                    S.emit("tensor", lambda E, s=slot, st=st_, sp=sp_: E.matmul(
                        B[4][0:1, :], c1cb[:, 0:1], xbfs[:, s * 1024:s * 1024 + 512], start=st, stop=sp))
                    S.emit("tensor", lambda E, s=slot, st=st_, sp=sp_: E.matmul(
                        B[4][32:33, :], c1cb[:, 0:1], xbfs[:, s * 1024 + 512:(s + 1) * 1024], start=st, stop=sp,
                        tile_position=(0, 32)))
                    S.emit("tensor", lambda E, s=slot, st=st_, sp=sp_: E.matmul(
                        B[5][0:1, :], c1cb[:, 0:1], sqbfs[:, s * 1024:s * 1024 + 512], start=st, stop=sp))
                    stats2_pe[(t, c)] = S.emit("tensor", lambda E, s=slot, st=st_, sp=sp_: E.matmul(
                        B[5][32:33, :], c1cb[:, 0:1], sqbfs[:, s * 1024 + 512:(s + 1) * 1024], start=st, stop=sp,
                        tile_position=(0, 32)), "PE")

                if pending_stats2 is not None:
                    pending_stats2()
                pending_stats2 = emit_stats2

            pending_stats2()
            pending_stats2 = None
            bc2 = ln_finalize(t, 1)
            bcast_pe[(t, 1)] = bc2[1]

            for h in range(2):
                aB = B[6] if h == 0 else B[4]
                dB = B[7] if h == 0 else B[5]
                for cp in range(0, NDM, 2):
                    S.wait("vector", "PE", bc2[h])
                    for c in (cp, cp + 1):
                        S.emit("vector", lambda E, c=c, h=h, aB=aB, r=(c % 2) * 512: E.tensor_tensor(
                            lnscr[:, r:r + 512],
                            xbuf[:, c * 1024 + h * 512: c * 1024 + (h + 1) * 512],
                            aB[:, :], OP.mult), drain=(c == cp))
                    for c in (cp, cp + 1):
                        apply_ve[(t, 1, c, h)] = S.emit(
                            "vector", lambda E, c=c, h=h, dB=dB, r=(c % 2) * 512: E.tensor_tensor(
                                xlnb[:, c * 1024 + h * 512: c * 1024 + (h + 1) * 512],
                                lnscr[:, r:r + 512], dB[:, :], OP.subtract), "VE", drain=(c == cp))
                    for c in (cp, cp + 1):
                        S.wait("scalar", "VE", apply_ve[(t, 1, c, h)])
                        if t > 0 and c == 0 and h == 0:
                            S.wait("scalar", "PE", m4_pe[(t - 1, 7)])
                        quant_sc[(t, c, h)] = S.emit("scalar", lambda E, c=c, h=h: E.activation(
                            qb[:, c * 1024 + h * 512:c * 1024 + (h + 1) * 512],
                            xlnb[:, c * 1024 + h * 512:c * 1024 + (h + 1) * 512],
                            AF.Copy, scale=oncf[:, 1:2]), "SC")

            for f in range(8):
                pair = (B[0], B[1]) if f % 2 == 0 else (B[2], B[3])
                if f >= 2:
                    S.wait("tensor", "SC", m4evac_sc[(t, f - 2)])
                else:
                    S.wait("tensor", "VE", resid_ve[(t, 6 + f)])
                r = None
                if f == 0:
                    for h in range(2):
                        for d in range(NDM):
                            st_, sp_ = (d == 0), (d == NDM - 1)
                            S.wait("tensor", "SC", quant_sc[(t, d, h)])
                            lhs = kqb[:, d * 1024 + f * 128: d * 1024 + (f + 1) * 128]
                            r = S.emit("tensor", lambda E, l=lhs, p=pair[h], d=d, h=h, st=st_, sp=sp_: E.matmul(
                                p[:, :], l, qb[:, d * 1024 + h * 512: d * 1024 + (h + 1) * 512],
                                start=st, stop=sp), "PE" if (sp_ and h == 1) else None)
                else:
                    for d in range(NDM):
                        st_, sp_ = (d == 0), (d == NDM - 1)
                        S.wait("tensor", "SC", quant_sc[(t, d, 0)])
                        lhs = kqb[:, d * 1024 + f * 128: d * 1024 + (f + 1) * 128]
                        S.emit("tensor", lambda E, l=lhs, p=pair[0], d=d, st=st_, sp=sp_: E.matmul(
                            p[:, :], l, qb[:, d * 1024: d * 1024 + 512], start=st, stop=sp))
                        S.wait("tensor", "SC", quant_sc[(t, d, 1)])
                        r = S.emit("tensor", lambda E, l=lhs, p=pair[1], d=d, st=st_, sp=sp_: E.matmul(
                            p[:, :], l, qb[:, d * 1024 + 512: (d + 1) * 1024], start=st, stop=sp),
                            "PE" if sp_ else None)
                m4_pe[(t, f)] = r

                S.wait("scalar", "PE", m4_pe[(t, f)])
                S.wait("scalar", "CD", xd_consts)
                if f >= 1:
                    S.wait("scalar", "VE", final_ve[(t, f - 1)])
                elif t > 0:
                    S.wait("scalar", "VE", final_ve[(t - 1, 7)])
                fslot = f % 2
                r = None
                for h in range(2):
                    r = S.emit("scalar", lambda E, p=pair[h], f=f, h=h: E.activation(
                        ffns[:, h * 512:(h + 1) * 512],
                        p[:, :], AF.Identity, scale=oncf[:, 2:3], bias=fbs[:, f:f + 1]),
                        "SC" if h == 1 else None)
                m4evac_sc[(t, f)] = r

                S.wait("vector", "SC", m4evac_sc[(t, f)])
                if f >= 2:
                    S.wait("vector", f"OD{f - 2}", store_od[(t, f - 2)])
                elif t > 0:
                    S.wait("vector", f"OD{6 + f}", store_od[(t - 1, 6 + f)])
                final_ve[(t, f)] = S.emit("vector", lambda E, f=f, fs=fslot: E.tensor_tensor(
                    outst[:, fs * 1024:(fs + 1) * 1024],
                    xbuf[:, f * 1024:(f + 1) * 1024],
                    ffns[:, 0:1024], OP.add), "VE")
                S.wait("sync", "VE", final_ve[(t, f)])
                store_od[(t, f)] = dma(
                    "sync", dout[f * 128:(f + 1) * 128, t * TO:(t + 1) * TO],
                    outst[:, fslot * 1024:(fslot + 1) * 1024], f"OD{f}")

                if t + 1 < NT:
                    ln1_xload(t + 1, f)

            if t + 1 < NT:
                for cc in range(4):
                    w1_dma(t + 1, cc)
                for cc in range(NDM):
                    ln1_cast(t + 1, cc)
                    ln1_statmm(t + 1, cc)
                ln1_tail(t + 1)

        for f in range(8):
            S.wait("sync", f"OD{f}", S.cnt[f"OD{f}"])

        def make_body(items):
            def body(E):
                watermark = {}
                for it in items:
                    if it[0] == "w":
                        _, sem, val = it
                        if val > watermark.get(sem, 0):
                            E.wait_ge(sems[sem], val)
                            watermark[sem] = val
                    else:
                        _, fn, sem, amt, dr = it
                        if dr:
                            E.drain(fusable=True)
                        inst = fn(E)
                        if sem is not None:
                            inst.then_inc(sems[sem], amt)
            return body

        for name in ENGINES:
            if S.q[name]:
                getattr(block, name)(make_body(S.q[name]))

    return nc


def _host_layouts(inputs):
    x = np.asarray(inputs["L0_attn_x"])
    W_in = np.asarray(inputs["L0_attn_W_in"])
    conv_w = np.asarray(inputs["L0_attn_conv_w"])
    conv_b = np.asarray(inputs["L0_attn_conv_b"])
    W_xp = np.asarray(inputs["L0_attn_W_xproj"])
    D = np.asarray(inputs["L0_attn_D"])
    W_out = np.asarray(inputs["L0_attn_W_out"])
    fk = np.asarray(inputs["L0_ffn_kernel"])
    s_in = np.asarray(inputs["L0_ffn_input_scale"])
    s_k = np.asarray(inputs["L0_ffn_kernel_scale"])
    fb = np.asarray(inputs["L0_ffn_bias"])

    W4 = np.ascontiguousarray(W_in.reshape(32, 128, 8, 128).transpose(0, 3, 2, 1))
    w1_all = W4.reshape(32, 128, 1024)
    diag = np.zeros((16, 128, 4, 128), np.float32)
    idx = np.arange(128)
    for c in range(16):
        for k in range(4):
            diag[c, idx, k, idx] = conv_w[c * 128 + idx, k]
    w1a = np.ascontiguousarray(
        np.concatenate([w1_all[:16], diag.reshape(16, 128, 512)], axis=2))
    w1b = np.ascontiguousarray(w1_all[16:])
    w2 = np.ascontiguousarray(
        W_xp[32:].reshape(16, 128, 16, 128).transpose(0, 3, 2, 1).reshape(16, 128, 2048))
    wbc = np.ascontiguousarray(
        np.stack([W_xp[:32, dd * 128:(dd + 1) * 128].T for dd in range(16)], axis=1).reshape(128, 512))
    w3 = np.ascontiguousarray(
        W_out.reshape(8, 128, 16, 128).transpose(0, 3, 2, 1).reshape(8, 128, 2048))
    fk_t = np.ascontiguousarray(fk.reshape(8, 128, 1024))
    cb = np.ascontiguousarray(conv_b.reshape(16, 128).T)
    dm = np.ascontiguousarray(D.reshape(16, 128).T)
    fbm = np.ascontiguousarray(fb.reshape(8, 128).T)
    scales = np.array([[float(s_in[0]), float(s_k[0])]], np.float32)
    shared = {
        "w1a": w1a, "w1b": w1b, "w2": w2, "wbc": wbc, "w3": w3, "fk": fk_t,
        "cb": cb, "dmat": dm, "fbias": fbm, "scales": scales,
        "onesr": np.ones((1, 128), np.float32),
        "onescf": np.ones((128, 1), np.float32),
        "c1cb": np.full((128, 1), 1.0 / DM, BF),
    }
    in_maps = []
    for b in range(NCORES):
        m = dict(shared)
        m["x"] = np.ascontiguousarray(x[b].T)
        in_maps.append(m)
    return in_maps


def kernel(**inputs):
    in_maps = _host_layouts(inputs)
    nc = build_nc()
    res = run_bass_kernel_spmd(nc, in_maps, core_ids=list(range(NCORES)))
    out = np.stack([np.asarray(res.results[b]["out"]).T for b in range(NCORES)], axis=0)
    return np.ascontiguousarray(out.astype(np.float32))
